# version 02
# baseline (speedup 1.0000x reference)
import numpy as np
import concourse.bass as bass
import concourse.mybir as mybir
from concourse.bass_utils import run_bass_kernel_spmd

F32 = mybir.dt.float32
BF16 = mybir.dt.bfloat16
I32 = mybir.dt.int32
AF = mybir.ActivationFunctionType
ALU = mybir.AluOpType

B, L, D, H, DK, M = 16, 2048, 1024, 16, 64, 256
NC = 8
EPS = 1e-6
LNEPS = 1e-5
PHI_SCALE = float(DK) ** -0.25
LNEPS_BIAS = float(np.log(EPS))
NB = 8
TOK = 256
LT = 256
NLT = L // LT
PT = 128
NPT = L // PT
VS = 65
KVW = H * 2 * VS

VSA, VCA, VGA, VSB, VCB, VGB = 0, 1, 2, 3, 4, 5


class Sem:
    def __init__(self, nc, name):
        self.h = nc.semaphore(name)
        self.sem = self.h.__enter__()
        self.v = 0

    def inc(self, instr, k=1):
        instr.then_inc(self.sem, k)
        self.v += k
        return self.v

    def wait(self, eng, v=None):
        t = self.v if v is None else v
        if t > 0:
            eng.wait_ge(self.sem, t)


class MK:
    def __init__(self, taps=(), stage="full", a2_nb=B, a2_upto="full"):
        self.taps = set(taps)
        self.stage = stage
        self.a2_nb = a2_nb
        self.a2_upto = a2_upto
        self.nc = bass.Bass("TRN2", target_bir_lowering=False, debug=False)
        self.ctx = []
        self.tap_t = {}
        self.dt_in = {}

    def din(self, name, shape, dt=F32):
        self.dt_in[name] = self.nc.dram_tensor(name, list(shape), dt, kind="ExternalInput").ap()
        return self.dt_in[name]

    def sb(self, name, shape, dt):
        t = self.nc.sbuf_tensor(name, list(shape), dt)
        self.ctx.append(t)
        return t.__enter__()

    def sem(self, name):
        s = Sem(self.nc, name)
        self.ctx.append(s.h)
        return s

    def tapout(self, name, shape, dt=F32):
        if name in self.taps:
            self.tap_t[name] = self.nc.dram_tensor(
                "tap_" + name, list(shape), dt, kind="ExternalOutput").ap()
            return self.tap_t[name]
        return None

    def tap_sbuf(self, name, src_ap, shape, dt=F32, wait_sems=()):
        t = self.tapout(name, shape, dt)
        if t is None:
            return
        SY = self.nc.sync
        for s in wait_sems:
            s.wait(SY)
        v = self.s_tap.inc(SY.dma_start(out=t, in_=src_ap), 16)
        self.s_tap.wait(SY, v)

    def build(self):
        nc = self.nc
        din = self.din
        din("xkv", (NB, D, L)); din("xq", (B, D, TOK))
        din("wqT", (D, D)); din("wkT", (D, D)); din("wvT", (D, D)); din("woT", (D, D))
        din("wmod", (D, 6 * D)); din("orfT2", (128, 2 * M))
        din("cT", (128, 128)); din("cT2", (128, 128))
        din("bqc", (128, NC)); din("bkc", (128, NC)); din("boc", (128, NC))
        din("bv_row", (1, D))
        for nm in ("wfT", "ufT", "wiT", "uiT", "wcT", "ucT", "wo2T", "uo2T"):
            din(nm, (D, D))
        din("bfc", (128, NC)); din("bic", (128, NC)); din("bcc", (128, NC))
        din("bo2c", (128, NC)); din("incc", (128, NC))
        din("masks", (NC, 128, H))
        din("ones_row", (1, 128))
        din("ones_col", (128, 1))
        din("eps_row", (1, 128))
        din("pairmask", (2, 128))
        din("flagrows", (NB, 128), I32)
        din("epsbc_i", (128, TOK))
        din("negq_i", (128, 128))
        din("lneps_i", (128, 1))
        din("vones_i", (128, H))
        din("c0init", (128, NC * TOK))

        self.out = nc.dram_tensor("out", [B, NC, 128, TOK], F32, kind="ExternalOutput").ap()
        self.sh_kv = nc.dram_tensor("sh_kv", [2, NB, 128, KVW], BF16, kind="Internal",
                                    addr_space="Shared").ap()
        self.sh_flag = nc.dram_tensor("sh_flag", [2, 128], I32, kind="Internal",
                                      addr_space="Shared").ap()
        self.h2_dram = nc.dram_tensor("h2_dram", [B, NC, 128, TOK], BF16, kind="Internal").ap()
        self.xpost_dram = nc.dram_tensor("xpost_dram", [B, NC, 128, TOK], F32, kind="Internal").ap()
        self.w_dram = nc.dram_tensor("w_dram", [B, H, TOK], BF16, kind="Internal").ap()
        self.dens_dram = nc.dram_tensor("dens_dram", [B, H, TOK], F32, kind="Internal").ap()
        self.rds_dram = nc.dram_tensor("rds_dram", [B, H, TOK], BF16, kind="Internal").ap()

        blk = self.nc.Block()
        self.ctx.append(blk)
        blk.__enter__()
        self.TE, self.VE, self.SE, self.PE, self.SY = nc.tensor, nc.vector, nc.scalar, nc.gpsimd, nc.sync
        for s in ("s_ld", "s_pe", "s_act", "s_dve", "s_pool", "s_kvw", "s_shf",
                  "s_poll", "s_kvr", "s_st", "s_out", "s_tap", "s_x", "s_wld", "s_h2",
                  "s_xp", "s_cstep"):
            setattr(self, s, self.sem(s))

        self.pb = []
        for i in range(8):
            t = self.nc.psum_tensor("pb%d" % i, [128, 512], F32)
            self.ctx.append(t)
            self.pb.append(t.__enter__())

        self.poll_reg_ctx = self.PE.register("poll_tgt")
        self.poll_tgt = self.poll_reg_ctx.__enter__()
        self.poll_go_ctx = self.PE.register("poll_go")
        self.poll_go = self.poll_go_ctx.__enter__()
        self.PE.reg_mov(self.poll_tgt, 0)

        self.phase_consts()
        if self.stage in ("a1", "a2", "full"):
            self.phase_a1()
        if self.stage in ("a2", "full"):
            self.phase_barrier()
            self.phase_a2()
        if self.stage in ("full",):
            self.phase_barrier()
            self.phase_f()
        self.finalize()

        self.poll_go_ctx.__exit__(None, None, None)
        self.poll_reg_ctx.__exit__(None, None, None)
        for t in reversed(self.ctx):
            t.__exit__(None, None, None)
        return nc

    def phase_consts(self):
        nc, sb = self.nc, self.sb
        TE, VE, SE, SY = self.TE, self.VE, self.SE, self.SY
        din = self.dt_in
        s_ld, s_pe, s_act, s_dve = self.s_ld, self.s_pe, self.s_act, self.s_dve

        self.modT = sb("modT", [128, 48 * B], F32)
        self.modTl = sb("modTl", [128, 2 * NC * B], F32)
        self.orf_bf = sb("orf_bf", [128, 2 * M], BF16)
        self.ones_bf = sb("ones_bf", [1, 128], BF16)
        self.onesc_bf = sb("onesc_bf", [128, 1], BF16)
        self.epsr_bf = sb("epsr_bf", [1, 128], BF16)
        self.pairm_bf = sb("pairm_bf", [2, 128], BF16)
        self.mask_bf = sb("mask_bf", [128, NC * H], BF16)
        self.colsq = sb("colsq", [128, 8 * NC], F32)
        self.bvb = sb("bvb", [128, D], F32)
        self.flag_sb = sb("flag_sb", [NB, 128], I32)
        self.lneps_col = sb("lneps_col", [128, 1], F32)
        self.epsbc_bf = sb("epsbc_bf", [128, TOK], BF16)
        self.negq_bf = sb("negq_bf", [128, 128], BF16)
        self.fl_tmp = sb("fl_tmp", [1, 128], I32)

        stctx = []
        def sbl(name, shape, dt):
            t = nc.sbuf_tensor(name, list(shape), dt)
            stctx.append(t)
            return t.__enter__()
        orf_f32 = sbl("orf_f32", [128, 2 * M], F32)
        cT_sb = sbl("cT_sb", [128, 128], F32)
        cT2_sb = sbl("cT2_sb", [128, 128], F32)
        bvr_sb = sbl("bvr_sb", [1, D], F32)
        bvr_bf = sbl("bvr_bf", [1, D], BF16)

        s_ld.inc(SY.dma_start(out=orf_f32[:], in_=din["orfT2"]), 16)
        s_ld.inc(SY.dma_start(out=cT_sb[:], in_=din["cT"]), 16)
        s_ld.inc(SY.dma_start(out=cT2_sb[:], in_=din["cT2"]), 16)
        ebc_f32 = sbl("ebc_f32", [128, TOK], F32)
        ngq_f32 = sbl("ngq_f32", [128, 128], F32)
        cst_f32 = sbl("cst_f32", [128, NC * H + 128 + 1 + 128 + 2 * 128], F32)
        o = 0
        cst_slices = {}
        for nm, rows, cols in (("masks", 128, NC * H), ("ones_row", 1, 128),
                               ("ones_col", 128, 1), ("eps_row", 1, 128),
                               ("pairmask", 2, 128)):
            cst_slices[nm] = (rows, o, cols)
            if nm == "masks":
                s_ld.inc(SY.dma_start(out=cst_f32[0:rows, o:o + cols].rearrange("p (c h) -> p c h", h=H),
                                      in_=din[nm].rearrange("c p h -> p c h")), 16)
            else:
                s_ld.inc(SY.dma_start(out=cst_f32[0:rows, o:o + cols], in_=din[nm]), 16)
            o += cols
        s_ld.inc(SY.dma_start(out=self.flag_sb[:], in_=din["flagrows"]), 16)
        s_ld.inc(SY.dma_start(out=ebc_f32[:], in_=din["epsbc_i"]), 16)
        s_ld.inc(SY.dma_start(out=ngq_f32[:], in_=din["negq_i"]), 16)
        for j, nm in enumerate(["bqc", "bkc", "boc", "bfc", "bic", "bcc", "bo2c", "incc"]):
            s_ld.inc(SY.dma_start(out=self.colsq[:, j * NC:(j + 1) * NC], in_=din[nm]), 16)
        s_ld.inc(SY.dma_start(out=bvr_sb[:], in_=din["bv_row"]), 16)

        s_ld.inc(SY.dma_start(out=self.lneps_col[:], in_=din["lneps_i"]), 16)
        ld_consts = s_ld.v
        s_ld.wait(VE, ld_consts)
        s_dve.inc(VE.tensor_copy(out=self.orf_bf[:], in_=orf_f32[:]))
        for nm, dst in (("masks", self.mask_bf), ("ones_row", self.ones_bf),
                        ("ones_col", self.onesc_bf), ("eps_row", self.epsr_bf),
                        ("pairmask", self.pairm_bf)):
            rows, o, cols = cst_slices[nm]
            s_dve.inc(VE.tensor_copy(out=dst[0:rows, 0:cols], in_=cst_f32[0:rows, o:o + cols]))
        s_dve.inc(VE.tensor_copy(out=self.epsbc_bf[:], in_=ebc_f32[:]))
        s_dve.inc(VE.tensor_copy(out=self.negq_bf[:], in_=ngq_f32[:]))
        v_bv = s_dve.inc(VE.tensor_copy(out=bvr_bf[:], in_=bvr_sb[:]))
        self.dve_consts = s_dve.v

        s_ld.wait(SE, ld_consts)
        siluc = sbl("siluc", [128, 128], F32)
        siluc2 = sbl("siluc2", [128, 128], F32)
        SE.activation(out=siluc[:], in_=cT_sb[:], func=AF.Silu)
        a_sil = s_act.inc(SE.activation(out=siluc2[:], in_=cT2_sb[:], func=AF.Silu))

        p_bv = self.pb[6]
        s_dve.wait(TE, self.dve_consts)
        for hh in range(2):
            mm = TE.matmul(p_bv[:, 0:512], self.ones_bf[:],
                           bvr_bf[0:1, hh * 512:(hh + 1) * 512], start=True, stop=True)
            pv = s_pe.inc(mm)
            s_pe.wait(SE, pv)
            a = SE.activation(out=self.bvb[:, hh * 512:(hh + 1) * 512],
                              in_=p_bv[:, 0:512], func=AF.Copy)
            s_act.inc(a)
            s_act.wait(TE)
        self.act_bvb = self.s_act.v

        wm_a = sbl("wm_a", [128, NC * 512], F32)
        wm_b = sbl("wm_b", [128, NC * 512], F32)
        p_mod = self.pb[7]
        pe_done_sg = {}
        for sg in range(12):
            buf = wm_a if sg % 2 == 0 else wm_b
            if sg >= 2:
                s_pe.wait(SY, pe_done_sg[sg - 2])
            ld = SY.dma_start(out=buf[:].rearrange("p (c j) -> p c j", j=512),
                              in_=din["wmod"].rearrange("(c p) j -> p c j", p=128)[:, :, sg * 512:(sg + 1) * 512])
            lv = s_ld.inc(ld, 16)
            s_ld.wait(TE, lv)
            if sg == 0:
                s_act.wait(TE, a_sil)
            for s4 in range(4):
                s_glob = sg * 4 + s4
                dsts = [(self.modT[:, s_glob * B:(s_glob + 1) * B], cT_sb)]
                if s_glob < 2 * NC:
                    dsts.append((self.modTl[:, s_glob * B:(s_glob + 1) * B], cT2_sb))
                for dst, csrc in dsts:
                    csb = siluc if csrc is cT_sb else siluc2
                    for c in range(NC):
                        mm = TE.matmul(p_mod[:, s4 * 128:s4 * 128 + B],
                                       buf[:, c * 512 + s4 * 128:c * 512 + s4 * 128 + 128],
                                       csb[:, c * 16:(c + 1) * 16],
                                       start=(c == 0), stop=(c == NC - 1))
                    pv = s_pe.inc(mm)
                    s_pe.wait(SE, pv)
                    a = SE.activation(out=dst, in_=p_mod[:, s4 * 128:s4 * 128 + B], func=AF.Copy)
                    s_act.inc(a)
                    s_act.wait(TE)
            pe_done_sg[sg] = s_pe.v

        act_mod_raw = self.s_act.v
        s_act.wait(VE, act_mod_raw)
        for vec in (VCA, VCB):
            r0 = vec * NC * B
            s_dve.inc(VE.tensor_scalar(out=self.modT[:, r0:r0 + NC * B],
                                       in0=self.modT[:, r0:r0 + NC * B],
                                       scalar1=1.0, scalar2=None, op0=ALU.add))
        s_dve.inc(VE.tensor_scalar(out=self.modTl[:, NC * B:2 * NC * B],
                                   in0=self.modTl[:, NC * B:2 * NC * B],
                                   scalar1=1.0, scalar2=None, op0=ALU.add))
        gtmp = sbl("gtmp", [128, NC * B], F32)
        gsgn = sbl("gsgn", [128, NC * B], F32)
        for vec in (VGA, VGB):
            r0 = vec * NC * B
            g = self.modT[:, r0:r0 + NC * B]
            if vec != VGA:
                s_dve.wait(SE)
            SE.activation(out=gsgn[:], in_=g, func=AF.Sign)
            s_act.inc(SE.activation(out=gtmp[:], in_=g, func=AF.Abs))
            s_act.wait(VE)
            s_dve.inc(VE.tensor_scalar(out=gtmp[:], in0=gtmp[:], scalar1=0.25,
                                       scalar2=None, op0=ALU.max))
            s_dve.inc(VE.tensor_tensor(out=g, in0=gtmp[:], in1=gsgn[:], op=ALU.mult))
        self.dve_mod = self.s_dve.v
        self.tap_sbuf("modT", self.modT[:], (128, 48 * B), wait_sems=[self.s_dve])
        self.s_dve.wait(self.SY)
        self.s_act.wait(self.SY)
        self.s_pe.wait(self.SY)
        for t in reversed(stctx):
            t.__exit__(None, None, None)
        self.tap_sbuf("mask0", self.mask_bf[:], (128, NC * H), BF16, wait_sems=[self.s_dve])
        self.tap_sbuf("lneps0", self.lneps_col[:], (128, 1), wait_sems=[self.s_dve])

    def phase_barrier(self):
        for eng in (self.TE, self.VE, self.SE, self.PE, self.SY):
            for s in (self.s_pe, self.s_act, self.s_dve, self.s_x, self.s_wld,
                      self.s_st, self.s_kvw, self.s_kvr, self.s_h2, self.s_xp,
                      self.s_ld):
                s.wait(eng)

    def mcol(self, vec, c, b, local=False):
        s = vec * NC + c
        t = self.modTl if local else self.modT
        return t[:, s * B + b: s * B + b + 1]

    def col(self, name, c):
        j = {"bq": 0, "bk": 1, "bo": 2, "bf": 3, "bi": 4, "bc": 5, "bo2": 6, "inc": 7}[name]
        return self.colsq[:, j * NC + c: j * NC + c + 1]

    def emit_ln(self, ntok, sa_col, ca_col, p_stat, p_rmr, xt, xbf, x2bf, zt,
                h_out, rows, r_bf, extra_pe_wait=0):
        TE, VE, SE = self.TE, self.VE, self.SE
        s_pe, s_act, s_dve = self.s_pe, self.s_act, self.s_dve
        SE.activation(out=xbf[:, 0:NC * ntok], in_=xt[:, 0:NC * ntok], func=AF.Copy)
        a2 = SE.activation(out=x2bf[:, 0:NC * ntok], in_=xbf[:, 0:NC * ntok], func=AF.Square)
        av = s_act.inc(a2)
        s_act.wait(TE, av)
        for c in range(NC):
            mm = TE.matmul(p_stat[0:1, 0:ntok], self.onesc_bf[:],
                           xbf[:, c * ntok:(c + 1) * ntok], start=(c == 0), stop=(c == NC - 1))
        for c in range(NC):
            mm = TE.matmul(p_stat[0:1, 256:256 + ntok], self.onesc_bf[:],
                           x2bf[:, c * ntok:(c + 1) * ntok], start=(c == 0), stop=(c == NC - 1))
        pv = s_pe.inc(mm)
        s_pe.wait(SE, pv)
        SE.activation(out=rows[0:1, 0:ntok], in_=p_stat[0:1, 0:ntok], func=AF.Copy, scale=1.0 / D)
        SE.activation(out=rows[0:1, 256:256 + ntok], in_=p_stat[0:1, 0:ntok], func=AF.Square, scale=1.0 / D)
        a = SE.activation(out=rows[0:1, 512:512 + ntok], in_=p_stat[0:1, 256:256 + ntok],
                          func=AF.Copy, scale=1.0 / D, bias=LNEPS)
        av = s_act.inc(a)
        s_act.wait(VE, av)
        v = VE.tensor_tensor(out=rows[0:1, 512:512 + ntok], in0=rows[0:1, 512:512 + ntok],
                             in1=rows[0:1, 256:256 + ntok], op=ALU.subtract)
        dv = s_dve.inc(v)
        s_dve.wait(SE, dv)
        SE.activation(out=rows[0:1, 768:768 + ntok], in_=rows[0:1, 512:512 + ntok], func=AF.Ln)
        a = SE.activation(out=r_bf[0:1, 0:ntok], in_=rows[0:1, 768:768 + ntok], func=AF.Exp, scale=-0.5)
        av = s_act.inc(a)
        s_act.wait(VE, av)
        v = VE.tensor_tensor(out=r_bf[0:1, 256:256 + ntok], in0=rows[0:1, 0:ntok],
                             in1=r_bf[0:1, 0:ntok], op=ALU.mult)
        dv = s_dve.inc(v)
        s_dve.wait(TE, dv)
        if extra_pe_wait:
            pass
        mm = TE.matmul(p_rmr[:, 0:ntok], self.ones_bf[:], r_bf[0:1, 0:ntok], start=True, stop=True)
        mm = TE.matmul(p_rmr[:, 256:256 + ntok], self.ones_bf[:], r_bf[0:1, 256:256 + ntok],
                       start=True, stop=True)
        pv = s_pe.inc(mm)
        s_pe.wait(VE, pv)
        for c in range(NC):
            cs = c * ntok
            VE.tensor_tensor(out=zt[:, 0:ntok], in0=xbf[:, cs:cs + ntok],
                             in1=p_rmr[:, 0:ntok], op=ALU.mult)
            VE.tensor_tensor(out=zt[:, 0:ntok], in0=zt[:, 0:ntok],
                             in1=p_rmr[:, 256:256 + ntok], op=ALU.subtract)
            v = VE.tensor_scalar(out=h_out[:, cs:cs + ntok], in0=zt[:, 0:ntok],
                                 scalar1=ca_col(c), scalar2=sa_col(c),
                                 op0=ALU.mult, op1=ALU.add)
        return s_dve.inc(v)

    def phase_a1(self):
        nc = self.nc
        TE, VE, SE, PE, SY = self.TE, self.VE, self.SE, self.PE, self.SY
        s_x, s_wld, s_pe, s_act, s_dve = self.s_x, self.s_wld, self.s_pe, self.s_act, self.s_dve
        s_kvw, s_shf = self.s_kvw, self.s_shf
        din = self.dt_in
        pb = self.pb

        a1ctx = []
        def sba(name, shape, dt):
            t = nc.sbuf_tensor(name, list(shape), dt)
            a1ctx.append(t)
            return t.__enter__()

        wstage = sba("wstage", [128, D], F32)
        wk_bf = sba("wk_bf", [128, NC * D], BF16)
        wv_bf = sba("wv_bf", [128, NC * D], BF16)
        for nm, dst in (("wkT", wk_bf), ("wvT", wv_bf)):
            for ci in range(NC):
                lv = s_wld.inc(SY.dma_start(
                    out=wstage[:],
                    in_=din[nm].rearrange("(ci p) j -> ci p j", p=128)[ci]), 16)
                s_wld.wait(VE, lv)
                v = VE.tensor_copy(out=dst[:, ci * D:(ci + 1) * D], in_=wstage[:])
                dv = s_dve.inc(v)
                s_dve.wait(SY, dv)
        dve_w = self.s_dve.v

        xt_bufs = [sba("xt_a", [128, NC * LT], F32), sba("xt_b", [128, NC * LT], F32)]
        xbf = sba("xbf_a1", [128, NC * LT], BF16)
        x2bf = sba("x2bf_a1", [128, NC * LT], BF16)
        zt = sba("zt_a1", [128, LT], F32)
        rows = sba("rows_a1", [1, 1024], F32)
        r_bf = sba("rbf_a1", [1, 512], BF16)
        h_bf = sba("h_a1", [128, NC * LT], BF16)
        k_bf = sba("k_a1", [128, NC * L], BF16)
        v_aug = sba("v_a1", [128, NPT * H * VS], BF16)
        kphi = [sba("kphi_a", [128, 2 * M], BF16), sba("kphi_b", [128, 2 * M], BF16)]
        k2t = sba("k2t_a1", [128, NC * 128], BF16)
        negnk = sba("negnk_a1", [128, NPT * H], F32)
        vsum_bf = sba("vsum_a1", [1, H * VS], BF16)
        kv_stage = sba("kvst_a1", [128, KVW], BF16)

        par_g = PE.partition_id() % 2

        vones_f = sba("vones_f", [128, H], F32)
        vones_bf = sba("vones_bf", [128, H], BF16)
        vo_ld = self.s_x.inc(SY.dma_start(out=vones_f[:], in_=din["vones_i"]), 16)
        self.s_x.wait(VE, vo_ld)
        VE.tensor_copy(out=vones_bf[:], in_=vones_f[:])
        for tt in range(NPT):
            v = VE.tensor_copy(
                out=v_aug[:].rearrange("p (t hh v) -> p t hh v", v=VS, hh=H)[:, tt, :, 64:65]
                    .rearrange("p hh v -> p (hh v)"),
                in_=vones_bf[:])
        vo_dv = self.s_dve.inc(v)
        self.s_dve.wait(TE, vo_dv)
        kprev_evac = [0, 0]
        vprev_evac = [0, 0]
        xt_cast_done = [0, 0]
        ln_dve_prev = 0
        dve_h_prev = 0
        kphi_kv_done = [0, 0]
        self.a1_flag_vals = []

        for j in range(NB):
            for lt in range(NLT):
                it = j * NLT + lt
                xt = xt_bufs[it % 2]
                if it >= 2:
                    s_act.wait(SY, xt_cast_done[it % 2])
                lv = s_x.inc(SY.dma_start(
                    out=xt[:].rearrange("p (c t) -> p c t", t=LT),
                    in_=din["xkv"].rearrange("b (c p) l -> b p c l", p=128)[j][:, :, lt * LT:(lt + 1) * LT]), 16)
                self.s_x.wait(SE, lv)
                if it == 0:
                    self.s_dve.wait(SE, self.dve_mod)
                if it > 0:
                    s_pe.wait(VE, ln_dve_prev)
                    s_pe.wait(SE, ln_dve_prev)
                    s_dve.wait(SE, dve_h_prev)
                dve_h = self.emit_ln(LT,
                                     lambda c: self.mcol(VSA, c, j, local=True),
                                     lambda c: self.mcol(VCA, c, j, local=True),
                                     pb[0], pb[1], xt, xbf, x2bf, zt, h_bf, rows, r_bf)
                xt_cast_done[it % 2] = self.s_act.v
                s_dve.wait(TE, dve_h)
                if it == 0:
                    s_dve.wait(TE, dve_w)
                for co in range(NC):
                    pk = pb[2 + co % 2]
                    if kprev_evac[co % 2]:
                        s_dve.wait(TE, kprev_evac[co % 2])
                    for ci in range(NC):
                        mm = TE.matmul(pk[:, 0:LT],
                                       wk_bf[:, ci * D + co * 128: ci * D + (co + 1) * 128],
                                       h_bf[:, ci * LT:(ci + 1) * LT],
                                       start=(ci == 0), stop=(ci == NC - 1))
                    pv = s_pe.inc(mm)
                    s_pe.wait(VE, pv)
                    v = VE.tensor_scalar(out=k_bf[:, co * L + lt * LT: co * L + lt * LT + LT],
                                         in0=pk[:, 0:LT], scalar1=self.col("bk", co),
                                         scalar2=None, op0=ALU.add)
                    kprev_evac[co % 2] = s_dve.inc(v)
                for sub in range(LT // PT):
                    ptg = lt * (LT // PT) + sub
                    for half in range(2):
                        pv_ = pb[4 + half]
                        if vprev_evac[half]:
                            s_dve.wait(TE, vprev_evac[half])
                        for ci in range(NC):
                            mm = TE.matmul(pv_[:, 0:512],
                                           h_bf[:, ci * LT + sub * PT: ci * LT + sub * PT + PT],
                                           wv_bf[:, ci * D + half * 512: ci * D + (half + 1) * 512],
                                           start=(ci == 0), stop=(ci == NC - 1))
                        pev = s_pe.inc(mm)
                        s_pe.wait(VE, pev)
                        if it == 0 and sub == 0 and half == 0:
                            s_act.wait(VE, self.act_bvb)
                        v = VE.tensor_tensor(
                            out=v_aug[:, ptg * H * VS + half * 8 * VS:]
                                .rearrange("p (hh v) -> p hh v", v=VS)[:, 0:8, 0:64],
                            in0=pv_[:, 0:512].rearrange("p (hh d) -> p hh d", d=64),
                            in1=self.bvb[:, half * 512:(half + 1) * 512]
                                .rearrange("p (hh d) -> p hh d", d=64),
                            op=ALU.add)
                        vprev_evac[half] = s_dve.inc(v)
                ln_dve_prev = s_pe.v
                dve_h_prev = self.s_dve.v
            dve_v_done = self.s_dve.v
            s_dve.wait(TE, dve_v_done)
            for ptg in range(NPT):
                base = ptg * H * VS
                for si, (o, w) in enumerate([(0, 512), (512, 512), (1024, 16)]):
                    mm = TE.matmul(pb[si][0:1, 0:w], self.onesc_bf[:],
                                   v_aug[:, base + o: base + o + w],
                                   start=(ptg == 0), stop=(ptg == NPT - 1))
            pv = s_pe.inc(mm)
            s_pe.wait(SE, pv)
            SE.activation(out=vsum_bf[0:1, 0:512], in_=pb[0][0:1, 0:512], func=AF.Copy)
            SE.activation(out=vsum_bf[0:1, 512:1024], in_=pb[1][0:1, 0:512], func=AF.Copy)
            a = SE.activation(out=vsum_bf[0:1, 1024:1040], in_=pb[2][0:1, 0:16], func=AF.Copy)
            act_vsum = s_act.inc(a)
            s_act.wait(TE, act_vsum)

            for ptg in range(NPT):
                a = SE.activation(out=k2t[:].rearrange("p (c l) -> p c l", l=PT),
                                  in_=k_bf[:].rearrange("p (c l) -> p c l", l=L)[:, :, ptg * PT:(ptg + 1) * PT],
                                  func=AF.Square)
                av = s_act.inc(a)
                s_act.wait(TE, av)
                for c in range(NC):
                    mm = TE.matmul(pb[3][:, 0:16], k2t[:, c * 128:(c + 1) * 128],
                                   self.mask_bf[:, c * H:(c + 1) * H],
                                   start=(c == 0), stop=(c == NC - 1))
                pv = s_pe.inc(mm)
                s_pe.wait(SE, pv)
                a = SE.activation(out=negnk[:, ptg * H:(ptg + 1) * H], in_=pb[3][:, 0:16],
                                  func=AF.Copy, scale=-1.0 / 16.0)
                av = s_act.inc(a)
                s_act.wait(TE, av)
            act_nk = self.s_act.v
            s_act.wait(TE, act_nk)

            if j > 0:
                s_kvw.wait(SE, 16 * j)
            kv_evac_last = 0
            exp_done = [0, 0]
            for hp in range(NC):
                for ptg in range(NPT + 1):
                    if ptg < NPT:
                        pph = pb[4 + ptg % 2]
                        kb = kphi[ptg % 2]
                        if exp_done[ptg % 2]:
                            s_act.wait(TE, exp_done[ptg % 2])
                        mm = TE.matmul(pph[:, 0:512],
                                       k_bf[:, hp * L + ptg * PT: hp * L + ptg * PT + PT],
                                       self.orf_bf[:], start=True, stop=True)
                        pv = s_pe.inc(mm)
                        s_pe.wait(SE, pv)
                        if kphi_kv_done[ptg % 2]:
                            s_pe.wait(SE, kphi_kv_done[ptg % 2])
                        for hh in range(2):
                            h_glob = 2 * hp + hh
                            a = SE.activation(out=kb[:, hh * M:(hh + 1) * M],
                                              in_=pph[:, hh * M:(hh + 1) * M], func=AF.Exp,
                                              scale=PHI_SCALE,
                                              bias=negnk[:, ptg * H + h_glob: ptg * H + h_glob + 1])
                        exp_done[ptg % 2] = s_act.inc(a)
                    if ptg >= 1:
                        pt2 = ptg - 1
                        kb2 = kphi[pt2 % 2]
                        s_act.wait(TE, exp_done[pt2 % 2])
                        for hh in range(2):
                            h_glob = 2 * hp + hh
                            for mc in range(2):
                                gi = hh * 2 + mc
                                mm = TE.matmul(pb[gi][:, 0:VS],
                                               kb2[:, hh * M + mc * 128: hh * M + (mc + 1) * 128],
                                               v_aug[:, pt2 * H * VS + h_glob * VS: pt2 * H * VS + (h_glob + 1) * VS],
                                               start=(pt2 == 0), stop=False)
                        kphi_kv_done[pt2 % 2] = s_pe.inc(mm)
                for hh in range(2):
                    h_glob = 2 * hp + hh
                    for mc in range(2):
                        gi = hh * 2 + mc
                        mm = TE.matmul(pb[gi][:, 0:VS], self.epsr_bf[:],
                                       vsum_bf[0:1, h_glob * VS:(h_glob + 1) * VS],
                                       start=False, stop=True)
                        pv = s_pe.inc(mm)
                        s_pe.wait(SE, pv)
                        a = SE.activation(out=kv_stage[:, (h_glob * 2 + mc) * VS:(h_glob * 2 + mc + 1) * VS],
                                          in_=pb[gi][:, 0:VS], func=AF.Copy)
                        kv_evac_last = s_act.inc(a)
                        s_act.wait(TE, kv_evac_last)
            s_act.wait(PE, kv_evac_last)
            wv_ = s_kvw.inc(PE.dma_start(out=self.sh_kv[par_g][j], in_=kv_stage[:]), 16)
            s_kvw.wait(PE, wv_)
            fv_ = s_shf.inc(PE.dma_start(out=self.sh_flag[par_g].unsqueeze(0),
                                         in_=self.flag_sb[j:j + 1, :]), 16)
            if j == 0:
                self.tap_sbuf("kv0", kv_stage[:], (128, KVW), BF16, wait_sems=[self.s_act])
                self.tap_sbuf("vsum0", vsum_bf[:], (1, H * VS), BF16)
                self.tap_sbuf("negnk0", negnk[:], (128, NPT * H))
                self.tap_sbuf("k0", k_bf[:, 0:L], (128, L), BF16)
                self.tap_sbuf("v0", v_aug[:, 0:H * VS], (128, H * VS), BF16)
                self.tap_sbuf("h0", h_bf[:], (128, NC * LT), BF16)

        for t in reversed(a1ctx):
            t.__exit__(None, None, None)


    def phase_a2(self):
        nc = self.nc
        TE, VE, SE, PE, SY = self.TE, self.VE, self.SE, self.PE, self.SY
        s_x, s_wld, s_pe, s_act, s_dve = self.s_x, self.s_wld, self.s_pe, self.s_act, self.s_dve
        s_kvr, s_st, s_h2, s_xp = self.s_kvr, self.s_st, self.s_h2, self.s_xp
        din = self.dt_in
        pb = self.pb

        a2ctx = []
        def sba(name, shape, dt):
            t = nc.sbuf_tensor(name, list(shape), dt)
            a2ctx.append(t)
            return t.__enter__()

        wstage = sba("wstage2", [128, D], F32)
        wq_bf = sba("wq_bf", [128, NC * D], BF16)
        wo_bf = sba("wo_bf", [128, NC * D], BF16)
        for nm, dst in (("wqT", wq_bf), ("woT", wo_bf)):
            for ci in range(NC):
                lv = s_wld.inc(SY.dma_start(
                    out=wstage[:],
                    in_=din[nm].rearrange("(ci p) j -> ci p j", p=128)[ci]), 16)
                s_wld.wait(VE, lv)
                v = VE.tensor_copy(out=dst[:, ci * D:(ci + 1) * D], in_=wstage[:])
                dv = s_dve.inc(v)
                s_dve.wait(SY, dv)
        dve_w2 = self.s_dve.v

        dbg_sumq2 = sba("dbg_sumq2", [16, TOK], F32) if "sumq2" in self.taps else None
        xq_t = sba("xq_t", [128, NC * TOK], F32)
        xbf = sba("xbf_a2", [128, NC * TOK], BF16)
        x2bf = sba("x2bf_a2", [128, NC * TOK], BF16)
        zt = sba("zt_a2", [128, TOK], F32)
        rows = sba("rows_a2", [1, 1024], F32)
        r_bf = sba("rbf_a2", [1, 512], BF16)
        hq_bf = sba("hq_a2", [128, NC * TOK], BF16)
        q_bf = sba("q_a2", [128, NC * TOK], BF16)
        q2_bf = sba("q2_a2", [128, NC * TOK], BF16)
        qraw = [sba("qraw_a", [128, 2 * M], BF16), sba("qraw_b", [128, 2 * M], BF16)]
        kv_bf = [sba("kvbf_a", [128, KVW], BF16), sba("kvbf_b", [128, KVW], BF16)]
        num_bf = sba("num_a2", [128, NC * TOK], BF16)
        dens_sb = sba("dens_a2", [1, H * TOK], F32)
        dens16 = sba("dens16_a2", [16, TOK], F32)
        rds16 = sba("rds16_a2", [16, TOK], BF16)
        rds2 = sba("rds2_a2", [2, NC * TOK], BF16)
        rtmp = sba("rtmp_a2", [128, TOK], F32)
        xpost_t = sba("xpost_a2", [128, NC * TOK], F32)
        h2_bf = sba("h2_a2", [128, NC * TOK], BF16)

        num_done_b = [0, 0]
        prev_dve_resid = 0
        prev_pe_stats = 0
        prev_act = 0
        wbc_used_pe = 0
        qraw_used = [0, 0]
        UPTO = ["ln", "q", "w16", "wb", "w", "num", "div", "wo", "full"].index(self.a2_upto)
        for b in range(self.a2_nb):
            w_slot = b // NB
            j_loc = b % NB
            buf = b % 2
            PE.reg_mov(self.poll_go, 1)
            wg = PE.While(self.poll_go)
            wg.__enter__()
            PE.reg_add(self.poll_tgt, self.poll_tgt, 16)
            PE.dma_start(out=self.fl_tmp[:], in_=self.sh_flag[w_slot:w_slot + 1, :]).then_inc(
                self.s_poll.sem, 16)
            PE.wait_ge(self.s_poll.sem, self.poll_tgt)
            PE.load(self.poll_go, self.fl_tmp[0:1, 0:1])
            PE.reg_alu(self.poll_go, self.poll_go, j_loc + 1, ALU.is_lt)
            wg.__exit__(None, None, None)
            if num_done_b[buf]:
                s_pe.wait(PE, num_done_b[buf])
            kvv = s_kvr.inc(PE.dma_start(out=kv_bf[buf][:], in_=self.sh_kv[w_slot][j_loc]), 16)

            if b > 0:
                s_dve.wait(SY, prev_dve_resid)
            lv = s_x.inc(SY.dma_start(
                out=xq_t[:].rearrange("p (c t) -> p c t", t=TOK),
                in_=din["xq"].rearrange("b (c p) t -> b p c t", p=128)[b]), 16)
            s_x.wait(SE, lv)
            if b == 0:
                self.s_dve.wait(SE, self.dve_mod)
            else:
                s_pe.wait(SE, prev_pe_stats)
                s_dve.wait(SE, prev_dve_resid)
            dve_h = self.emit_ln(TOK,
                                 lambda c: self.mcol(VSA, c, b),
                                 lambda c: self.mcol(VCA, c, b),
                                 pb[0], pb[1], xq_t, xbf, x2bf, zt, hq_bf, rows, r_bf)
            if UPTO < 1:
                prev_dve_resid = self.s_dve.v
                prev_pe_stats = self.s_pe.v
                if b == 0:
                    self.tap_sbuf("hq0", hq_bf[:], (128, NC * TOK), BF16, wait_sems=[self.s_dve])
                continue
            s_dve.wait(TE, dve_h)
            if b == 0:
                s_dve.wait(TE, dve_w2)
            kprev = [0, 0]
            for co in range(NC):
                pk = pb[2 + co % 2]
                if kprev[co % 2]:
                    s_dve.wait(TE, kprev[co % 2])
                elif b > 0:
                    pass
                for ci in range(NC):
                    mm = TE.matmul(pk[:, 0:TOK],
                                   wq_bf[:, ci * D + co * 128: ci * D + (co + 1) * 128],
                                   hq_bf[:, ci * TOK:(ci + 1) * TOK],
                                   start=(ci == 0), stop=(ci == NC - 1))
                pv = s_pe.inc(mm)
                s_pe.wait(VE, pv)
                v = VE.tensor_scalar(out=q_bf[:, co * TOK:(co + 1) * TOK],
                                     in0=pk[:, 0:TOK], scalar1=self.col("bq", co),
                                     scalar2=None, op0=ALU.add)
                kprev[co % 2] = s_dve.inc(v)
            dve_q = self.s_dve.v
            if UPTO < 2:
                prev_dve_resid = self.s_dve.v
                prev_pe_stats = self.s_pe.v
                if b == 0:
                    self.tap_sbuf("hq0", hq_bf[:], (128, NC * TOK), BF16, wait_sems=[self.s_dve])
                    self.tap_sbuf("q0", q_bf[:], (128, NC * TOK), BF16)
                continue
            s_dve.wait(SE, dve_q)
            a = SE.activation(out=q2_bf[:], in_=q_bf[:], func=AF.Square,
                              scale=float(2.0 ** 0.25))
            av = s_act.inc(a)
            s_act.wait(TE, av)
            if UPTO < 5:
                prev_dve_resid = self.s_dve.v
                prev_pe_stats = self.s_pe.v
                if b == 0:
                    self.tap_sbuf("q20", q2_bf[:], (128, NC * TOK), BF16, wait_sems=[self.s_act])
                continue
            s_kvr.wait(TE, kvv)
            numev_prev = [0, 0]
            qexp_done = [0, 0]
            for h in range(H + 1):
                if h < H:
                    hb = h % 2
                    pph = pb[4 + hb]
                    if qexp_done[hb]:
                        s_act.wait(TE, qexp_done[hb])
                    hb2 = 64 * (h % 2)
                    for mc in range(2):
                        mm = TE.matmul(pph[:, mc * M:(mc + 1) * M],
                                       self.orf_bf[hb2:hb2 + 64,
                                                   (h % 2) * M + mc * 128:(h % 2) * M + (mc + 1) * 128],
                                       q_bf[hb2:hb2 + 64,
                                            (h // 2) * TOK:(h // 2 + 1) * TOK],
                                       start=True, stop=False)
                        mm = TE.matmul(pph[:, mc * M:(mc + 1) * M],
                                       self.negq_bf[hb2:hb2 + 64, 0:128],
                                       q2_bf[hb2:hb2 + 64,
                                             (h // 2) * TOK:(h // 2 + 1) * TOK],
                                       start=False, stop=True)
                    pv = s_pe.inc(mm)
                    s_pe.wait(SE, pv)
                    if qraw_used[hb]:
                        s_pe.wait(SE, qraw_used[hb])
                    a = SE.activation(out=qraw[hb][:], in_=pph[:, 0:512], func=AF.Exp,
                                      scale=PHI_SCALE)
                    qexp_done[hb] = s_act.inc(a)
                if h >= 1:
                    h2_ = h - 1
                    hb_ = h2_ % 2
                    s_act.wait(TE, qexp_done[hb_])
                    pn = pb[6 + hb_]
                    if numev_prev[hb_]:
                        s_act.wait(TE, numev_prev[hb_])
                    for mc in range(2):
                        sl = (2 * h2_ + mc) * VS
                        mm = TE.matmul(pn[0:VS, 0:TOK], kv_bf[buf][:, sl:sl + VS],
                                       qraw[hb_][:, mc * M:(mc + 1) * M][:, 0:TOK],
                                       start=(mc == 0), stop=False)
                        mm = TE.matmul(pn[0:VS, 0:TOK], kv_bf[buf][:, sl:sl + VS],
                                       self.epsbc_bf[:, 0:TOK],
                                       start=False, stop=(mc == 1))
                    qraw_used[hb_] = s_pe.inc(mm)
                    s_pe.wait(SE, qraw_used[hb_])
                    a = SE.activation(out=num_bf[64 * (h2_ % 2):64 * (h2_ % 2) + 64,
                                                 (h2_ // 2) * TOK:(h2_ // 2 + 1) * TOK],
                                      in_=pn[0:64, 0:TOK], func=AF.Copy)
                    a = SE.activation(out=dens_sb[0:1, h2_ * TOK:(h2_ + 1) * TOK],
                                      in_=pn[64:65, 0:TOK], func=AF.Copy)
                    numev_prev[hb_] = s_act.inc(a)
            num_done_b[buf] = self.s_pe.v
            act_nums = self.s_act.v

            if UPTO < 6:
                prev_dve_resid = self.s_dve.v
                prev_pe_stats = self.s_pe.v
                if b == 0:
                    self.tap_sbuf("num0", num_bf[:], (128, NC * TOK), BF16, wait_sems=[self.s_act])
                    self.tap_sbuf("dens0", dens_sb[:], (1, H * TOK))
                continue
            s_act.wait(SY, act_nums)
            sv = s_st.inc(SY.dma_start(out=self.dens_dram[b].rearrange("h t -> (h t)").unsqueeze(0),
                                       in_=dens_sb[:]), 16)
            s_st.wait(SY, sv)
            sv = s_st.inc(SY.dma_start(out=dens16[:], in_=self.dens_dram[b]), 16)
            s_st.wait(SE, sv)
            SE.activation(out=dens16[:], in_=dens16[:], func=AF.Ln)
            a = SE.activation(out=rds16[:], in_=dens16[:], func=AF.Exp, scale=-1.0)
            av = s_act.inc(a)
            s_act.wait(SY, av)
            sv = s_st.inc(SY.dma_start(out=self.rds_dram[b], in_=rds16[:]), 16)
            s_st.wait(SY, sv)
            sv = s_st.inc(SY.dma_start(out=rds2[:].rearrange("two (c t) -> two c t", t=TOK),
                                       in_=self.rds_dram[b].rearrange("(c two) t -> two c t", two=2)), 16)
            s_st.wait(TE, sv)
            div_prev = [0, 0]
            for hp in range(NC):
                pd = pb[2 + hp % 2]
                if div_prev[hp % 2]:
                    s_dve.wait(TE, div_prev[hp % 2])
                mm = TE.matmul(pd[:, 0:TOK], self.pairm_bf[:],
                               rds2[0:2, hp * TOK:(hp + 1) * TOK], start=True, stop=True)
                pv = s_pe.inc(mm)
                s_pe.wait(VE, pv)
                v = VE.tensor_tensor(out=num_bf[:, hp * TOK:(hp + 1) * TOK],
                                     in0=num_bf[:, hp * TOK:(hp + 1) * TOK],
                                     in1=pd[:, 0:TOK], op=ALU.mult)
                div_prev[hp % 2] = s_dve.inc(v)
            dve_div = self.s_dve.v
            if UPTO < 7:
                prev_dve_resid = self.s_dve.v
                prev_pe_stats = self.s_pe.v
                if b == 0:
                    self.tap_sbuf("attn0", num_bf[:], (128, NC * TOK), BF16, wait_sems=[self.s_dve])
                continue
            s_dve.wait(TE, dve_div)
            wo_prev = [0, 0]
            for co in range(NC):
                pw = pb[6 + co % 2]
                if wo_prev[co % 2]:
                    s_dve.wait(TE, wo_prev[co % 2])
                for hp in range(NC):
                    mm = TE.matmul(pw[:, 0:TOK],
                                   wo_bf[:, hp * D + co * 128: hp * D + (co + 1) * 128],
                                   num_bf[:, hp * TOK:(hp + 1) * TOK],
                                   start=(hp == 0), stop=(hp == NC - 1))
                pv = s_pe.inc(mm)
                s_pe.wait(VE, pv)
                VE.tensor_scalar(out=rtmp[:], in0=pw[:, 0:TOK],
                                 scalar1=self.col("bo", co), scalar2=self.mcol(VGA, co, b),
                                 op0=ALU.add, op1=ALU.mult)
                v = VE.tensor_tensor(out=xpost_t[:, co * TOK:(co + 1) * TOK],
                                     in0=xq_t[:, co * TOK:(co + 1) * TOK],
                                     in1=rtmp[:], op=ALU.mult if False else ALU.add)
                wo_prev[co % 2] = s_dve.inc(v)
            dve_xpost = self.s_dve.v
            prev_dve_resid = dve_xpost

            if UPTO < 8:
                if b == 0:
                    self.tap_sbuf("xpost0", xpost_t[:], (128, NC * TOK), wait_sems=[self.s_dve])
                prev_pe_stats = self.s_pe.v
                continue
            s_dve.wait(SE, dve_xpost)
            s_pe.wait(SE, self.s_pe.v)
            dve_h2 = self.emit_ln(TOK,
                                  lambda c: self.mcol(VSB, c, b),
                                  lambda c: self.mcol(VCB, c, b),
                                  pb[0], pb[1], xpost_t, xbf, x2bf, zt, h2_bf, rows, r_bf)
            prev_pe_stats = self.s_pe.v
            s_dve.wait(SY, dve_h2)
            s_h2.inc(SY.dma_start(out=self.h2_dram[b].rearrange("c p t -> p c t"),
                                  in_=h2_bf[:].rearrange("p (c t) -> p c t", t=TOK)), 16)
            s_xp.inc(SY.dma_start(out=self.xpost_dram[b].rearrange("c p t -> p c t"),
                                  in_=xpost_t[:].rearrange("p (c t) -> p c t", t=TOK)), 16)
            if b == 0:
                self.tap_sbuf("hq0", hq_bf[:], (128, NC * TOK), BF16, wait_sems=[self.s_dve])
                self.tap_sbuf("q0", q_bf[:], (128, NC * TOK), BF16)
                self.tap_sbuf("attn0", num_bf[:], (128, NC * TOK), BF16)
                self.tap_sbuf("xpost0", xpost_t[:], (128, NC * TOK))
                self.tap_sbuf("h2_0", h2_bf[:], (128, NC * TOK), BF16)
        self.a2_h2_total = self.s_h2.v
        self.a2_xp_total = self.s_xp.v

        for t in reversed(a2ctx):
            t.__exit__(None, None, None)


    def phase_f(self):
        nc = self.nc
        TE, VE, SE, PE, SY = self.TE, self.VE, self.SE, self.PE, self.SY
        s_x, s_wld, s_pe, s_act, s_dve = self.s_x, self.s_wld, self.s_pe, self.s_act, self.s_dve
        s_h2, s_xp, s_out = self.s_h2, self.s_xp, self.s_out
        din = self.dt_in
        pb = self.pb

        fctx = []
        def sba(name, shape, dt):
            t = nc.sbuf_tensor(name, list(shape), dt)
            fctx.append(t)
            return t.__enter__()

        wstage = sba("wstage_f", [128, D], F32)
        wgt = {}
        for nm in ("wfT", "wiT", "wcT", "ufT", "uiT", "ucT"):
            wgt[nm] = sba(nm + "_bf", [128, NC * D], BF16)
        for nm in ("wfT", "wiT", "wcT", "ufT", "uiT", "ucT"):
            for ci in range(NC):
                lv = s_wld.inc(SY.dma_start(
                    out=wstage[:],
                    in_=din[nm].rearrange("(ci p) j -> ci p j", p=128)[ci]), 16)
                s_wld.wait(VE, lv)
                v = VE.tensor_copy(out=wgt[nm][:, ci * D:(ci + 1) * D], in_=wstage[:])
                dv = s_dve.inc(v)
                s_dve.wait(SY, dv)
        dve_wf = self.s_dve.v

        cst = sba("c_f", [128, NC * TOK], F32)
        cbf = [sba("cbf_fa", [128, NC * TOK], BF16), sba("cbf_fb", [128, NC * TOK], BF16)]
        h2t = [sba("h2t_a", [128, NC * TOK], BF16), sba("h2t_b", [128, NC * TOK], BF16)]
        fgate = sba("fgate_f", [128, NC * TOK], F32)
        igate = sba("igate_f", [128, NC * TOK], F32)
        ggate = sba("ggate_f", [128, TOK], F32)
        ctmp = sba("ctmp_f", [128, TOK], F32)
        ctmp2 = sba("ctmp2_f", [128, TOK], F32)
        xpost_t = sba("xpostf", [128, NC * TOK], F32)
        out_t = sba("outf", [128, NC * TOK], F32)
        ht = sba("ht_f", [128, TOK], F32)

        c0_ld = s_x.inc(SY.dma_start(out=cst[:], in_=din["c0init"]), 16)
        s_x.wait(SE, c0_ld)
        a = SE.activation(out=cbf[0][:], in_=cst[:], func=AF.Copy)
        cast_done = s_act.inc(a)
        self.tap_sbuf("cbfinit", cbf[0][:], (128, NC * TOK), BF16, wait_sems=[self.s_act])
        self.tap_sbuf("cstinit", cst[:], (128, NC * TOK), wait_sems=[])

        dbg_g = sba("dbg_g", [128, TOK], F32) if "g00" in self.taps else None
        self.s_h2.wait(SY, 16 * 1)
        h2_ld = [0, 0]
        h2_ld[0] = s_x.inc(SY.dma_start(
            out=h2t[0][:].rearrange("p (c t) -> p c t", t=TOK),
            in_=self.h2_dram[0].rearrange("c p t -> p c t")), 16)

        VMAP = [("wfT", "ufT", "bf", AF.Sigmoid, fgate),
                ("wiT", "uiT", "bi", AF.Sigmoid, igate),
                ("wcT", "ucT", "bc", AF.Tanh, ggate)]
        bank_free = [0] * 8
        pe_step_end = [0, 0]
        dve_step_end = 0
        for t in range(B):
            if t + 1 < B:
                self.s_h2.wait(SY, 16 * (t + 2))
                if t >= 1:
                    s_pe.wait(SY, pe_step_end[(t + 1) % 2])
                h2_ld[(t + 1) % 2] = s_x.inc(SY.dma_start(
                    out=h2t[(t + 1) % 2][:].rearrange("p (c t) -> p c t", t=TOK),
                    in_=self.h2_dram[t + 1].rearrange("c p t -> p c t")), 16)
            s_x.wait(TE, h2_ld[t % 2])
            if t == 0:
                s_dve.wait(TE, dve_wf)
            for v_i, (wn, un, bn, func, dst) in enumerate(VMAP):
                for k in range(NC):
                    g = v_i * NC + k
                    bank = pb[g % 8]
                    if bank_free[g % 8]:
                        s_act.wait(TE, bank_free[g % 8])
                    for ci in range(NC):
                        mm = TE.matmul(bank[:, 0:TOK],
                                       wgt[wn][:, ci * D + k * 128: ci * D + (k + 1) * 128],
                                       h2t[t % 2][:, ci * TOK:(ci + 1) * TOK],
                                       start=(ci == 0), stop=False)
                if v_i == 0:
                    s_act.wait(TE, cast_done)
                for k in range(NC):
                    g = v_i * NC + k
                    bank = pb[g % 8]
                    for ci in range(NC):
                        mm = TE.matmul(bank[:, 0:TOK],
                                       wgt[un][:, ci * D + k * 128: ci * D + (k + 1) * 128],
                                       cbf[t % 2][:, ci * TOK:(ci + 1) * TOK],
                                       start=False, stop=(ci == NC - 1))
                    pv = s_pe.inc(mm)
                    s_pe.wait(SE, pv)
                    if t == 0 and v_i == 0 and k == 0 and "g00" in self.taps:
                        da = SE.activation(out=dbg_g[:], in_=bank[:, 0:TOK], func=AF.Copy)
                        self.s_act.inc(da)
                        self.tap_sbuf("g00", dbg_g[:], (128, TOK), wait_sems=[self.s_act])
                        self.tap_sbuf("cbf0", cbf[0][:], (128, NC * TOK), BF16)
                        self.tap_sbuf("h2t0", h2t[0][:], (128, NC * TOK), BF16)
                    if v_i < 2:
                        if t > 0 and k == 0 and v_i == 0:
                            s_dve.wait(SE, dve_step_end)
                        a = SE.activation(out=dst[:, k * TOK:(k + 1) * TOK], in_=bank[:, 0:TOK],
                                          func=func, bias=self.col(bn, k))
                        bank_free[g % 8] = s_act.inc(a)
                    else:
                        a = SE.activation(out=ggate[:], in_=bank[:, 0:TOK],
                                          func=func, bias=self.col(bn, k))
                        av = s_act.inc(a)
                        bank_free[g % 8] = av
                        s_act.wait(VE, av)
                        VE.tensor_tensor(out=ctmp[:], in0=fgate[:, k * TOK:(k + 1) * TOK],
                                         in1=cst[:, k * TOK:(k + 1) * TOK], op=ALU.mult)
                        VE.tensor_tensor(out=ctmp2[:], in0=igate[:, k * TOK:(k + 1) * TOK],
                                         in1=ggate[:], op=ALU.mult)
                        v = VE.tensor_tensor(out=cst[:, k * TOK:(k + 1) * TOK],
                                             in0=ctmp[:], in1=ctmp2[:], op=ALU.add)
                        dv = s_dve.inc(v)
                        s_dve.wait(SE, dv)
                        a = SE.activation(out=cbf[(t + 1) % 2][:, k * TOK:(k + 1) * TOK],
                                          in_=cst[:, k * TOK:(k + 1) * TOK], func=AF.Copy)
                        cast_done = s_act.inc(a)
            pe_step_end[t % 2] = self.s_pe.v
            dve_step_end = self.s_dve.v
            if t == 0:
                self.tap_sbuf("c1", cst[:], (128, NC * TOK), wait_sems=[self.s_dve])
                self.tap_sbuf("f0", fgate[:], (128, NC * TOK), wait_sems=[self.s_act])
                self.tap_sbuf("i0", igate[:], (128, NC * TOK))
        pe_scan_done = self.s_pe.v
        s_pe.wait(SY, pe_scan_done)
        for nm, dst in (("wo2T", wgt["wfT"]), ("uo2T", wgt["ufT"])):
            for ci in range(NC):
                lv = s_wld.inc(SY.dma_start(
                    out=wstage[:],
                    in_=din[nm].rearrange("(ci p) j -> ci p j", p=128)[ci]), 16)
                s_wld.wait(VE, lv)
                v = VE.tensor_copy(out=dst[:, ci * D:(ci + 1) * D], in_=wstage[:])
                dv = s_dve.inc(v)
                s_dve.wait(SY, dv)
        dve_wo2 = self.s_dve.v
        s_dve.wait(TE, dve_wo2)

        xp_ld = 0
        h2_ldb = 0
        fin_prev = [0, 0]
        out_dve_prev = 0
        for b in range(B):
            if b > 0:
                s_dve.wait(SY, out_dve_prev)
            self.s_xp.wait(SY, 16 * (b + 1))
            xp_ld = s_x.inc(SY.dma_start(
                out=xpost_t[:].rearrange("p (c t) -> p c t", t=TOK),
                in_=self.xpost_dram[b].rearrange("c p t -> p c t")), 16)
            h2_ldb = s_x.inc(SY.dma_start(
                out=h2t[0][:].rearrange("p (c t) -> p c t", t=TOK),
                in_=self.h2_dram[b].rearrange("c p t -> p c t")), 16)
            s_x.wait(TE, h2_ldb)
            for co in range(NC):
                bank = pb[co % 2]
                if fin_prev[co % 2]:
                    s_dve.wait(TE, fin_prev[co % 2])
                for ci in range(NC):
                    mm = TE.matmul(bank[:, 0:TOK],
                                   wgt["wfT"][:, ci * D + co * 128: ci * D + (co + 1) * 128],
                                   h2t[0][:, ci * TOK:(ci + 1) * TOK],
                                   start=(ci == 0), stop=False)
                for ci in range(NC):
                    mm = TE.matmul(bank[:, 0:TOK],
                                   wgt["ufT"][:, ci * D + co * 128: ci * D + (co + 1) * 128],
                                   cbf[B % 2][:, ci * TOK:(ci + 1) * TOK],
                                   start=False, stop=(ci == NC - 1))
                pv = s_pe.inc(mm)
                s_pe.wait(SE, pv)
                a = SE.activation(out=ht[:], in_=bank[:, 0:TOK], func=AF.Tanh,
                                  bias=self.col("bo2", co))
                av = s_act.inc(a)
                s_act.wait(VE, av)
                s_x.wait(VE, xp_ld)
                VE.tensor_scalar(out=ht[:], in0=ht[:], scalar1=self.mcol(VGB, co, b),
                                 scalar2=None, op0=ALU.mult)
                v = VE.tensor_tensor(out=out_t[:, co * TOK:(co + 1) * TOK],
                                     in0=xpost_t[:, co * TOK:(co + 1) * TOK],
                                     in1=ht[:], op=ALU.add)
                fin_prev[co % 2] = s_dve.inc(v)
            out_dve_prev = self.s_dve.v
            s_dve.wait(PE, out_dve_prev)
            self.s_out.inc(PE.dma_start(
                out=self.out[b].rearrange("c p t -> p c t"),
                in_=out_t[:].rearrange("p (c t) -> p c t", t=TOK)), 16)
            if b == 0:
                self.tap_sbuf("cfin", cst[:], (128, NC * TOK), wait_sems=[])
                self.tap_sbuf("out0", out_t[:], (128, NC * TOK), wait_sems=[self.s_dve])

        for t in reversed(fctx):
            t.__exit__(None, None, None)

    def finalize(self):
        self.s_kvw.wait(self.PE)
        self.s_shf.wait(self.PE)
        if self.stage == "full":
            self.s_out.wait(self.PE)
        self.s_h2.wait(self.SY)
        self.s_xp.wait(self.SY)


def host_inputs(inp, core):
    q = core % 2
    x = np.asarray(inp["L0_attn_x"])
    d = {}
    d["xkv"] = np.ascontiguousarray(x[8 * q:8 * q + 8].transpose(0, 2, 1))
    d["xq"] = np.ascontiguousarray(x[:, TOK * core:TOK * (core + 1), :].transpose(0, 2, 1))
    for nm, key in (("wqT", "L0_attn_Wq"), ("wkT", "L0_attn_Wk"), ("wvT", "L0_attn_Wv"),
                    ("woT", "L0_attn_Wo"), ("wfT", "L0_ffn_wf"), ("ufT", "L0_ffn_uf"),
                    ("wiT", "L0_ffn_wi"), ("uiT", "L0_ffn_ui"), ("wcT", "L0_ffn_wc"),
                    ("ucT", "L0_ffn_uc"), ("wo2T", "L0_ffn_wo"), ("uo2T", "L0_ffn_uo")):
        d[nm] = np.ascontiguousarray(np.asarray(inp[key]).T)
    d["wmod"] = np.ascontiguousarray(np.asarray(inp["L0_wmod"]))
    orfT = np.asarray(inp["L0_attn_orf"]).T
    o2 = np.zeros((128, 2 * M), np.float32)
    o2[0:64, 0:M] = orfT
    o2[64:128, M:2 * M] = orfT
    d["orfT2"] = o2

    def ct_layout(cmat):
        cc = np.ascontiguousarray(cmat.T)
        return np.ascontiguousarray(cc.reshape(NC, 128, cmat.shape[0]).transpose(1, 0, 2).reshape(128, 128))
    cfull = np.asarray(inp["L0_c"])
    d["cT"] = ct_layout(cfull)
    perm = list(range(8 * q, 8 * q + 8)) + list(range(8 * (1 - q), 8 * (1 - q) + 8))
    d["cT2"] = ct_layout(cfull[perm])
    for nm, key in (("bqc", "L0_attn_bq"), ("bkc", "L0_attn_bk"), ("boc", "L0_attn_bo"),
                    ("bfc", "L0_ffn_bf"), ("bic", "L0_ffn_bi"), ("bcc", "L0_ffn_bc"),
                    ("bo2c", "L0_ffn_bo"), ("incc", "L0_ffn_in_c")):
        d[nm] = np.ascontiguousarray(np.asarray(inp[key]).reshape(NC, 128).T)
    d["bv_row"] = np.ascontiguousarray(np.asarray(inp["L0_attn_bv"]).reshape(1, D))
    mk = np.zeros((NC, 128, H), np.float32)
    for ci in range(NC):
        mk[ci, 0:64, 2 * ci] = 1.0
        mk[ci, 64:128, 2 * ci + 1] = 1.0
    d["masks"] = mk
    d["ones_row"] = np.ones((1, 128), np.float32)
    d["ones_col"] = np.ones((128, 1), np.float32)
    d["eps_row"] = np.full((1, 128), EPS, np.float32)
    pm = np.zeros((2, 128), np.float32)
    pm[0, 0:64] = 1.0
    pm[1, 64:128] = 1.0
    d["pairmask"] = pm
    fr = np.zeros((NB, 128), np.int32)
    for j in range(NB):
        fr[j, :] = j + 1
    d["flagrows"] = fr
    d["epsbc_i"] = np.full((128, TOK), EPS, np.float32)
    d["negq_i"] = np.full((128, 128), -0.125, np.float32)
    d["lneps_i"] = np.full((128, 1), LNEPS_BIAS, np.float32)
    d["vones_i"] = np.ones((128, H), np.float32)
    inc_ = np.asarray(inp["L0_ffn_in_c"], np.float32)
    c0 = np.zeros((128, NC * TOK), np.float32)
    for k in range(NC):
        c0[:, k * TOK:(k + 1) * TOK] = inc_[k * 128:(k + 1) * 128, None]
    d["c0init"] = c0
    return d


import concourse.bass_utils as _bu

_CACHED = {}


def kernel(**inputs):
    if "nc" not in _CACHED:
        _CACHED["nc"] = MK(taps=(), stage="full").build()
    nc = _CACHED["nc"]
    in_maps = [host_inputs(inputs, c) for c in range(8)]
    res = _bu.run_bass_kernel_spmd(nc, in_maps, core_ids=list(range(8)))
    out = np.empty((B, L, D), np.float32)
    for c in range(8):
        oc = np.asarray(res.results[c]["out"]).reshape(B, D, TOK).transpose(0, 2, 1)
        out[:, TOK * c:TOK * (c + 1), :] = oc
    return out
